# version 05 (final)
# baseline (speedup 1.0000x reference)
import math
import contextlib
import numpy as np
import ml_dtypes
import concourse.bass as bass
import concourse.mybir as mybir
from concourse.bass_utils import run_bass_kernel_spmd

F32 = mybir.dt.float32
BF16 = mybir.dt.bfloat16
I32 = mybir.dt.int32
AF = mybir.ActivationFunctionType
ALU = mybir.AluOpType
AX = mybir.AxisListType
BF = ml_dtypes.bfloat16

S = 2048
D = 2048
NH = 16
II = 8192
NDC = D // 128
NST = S // 128
NIC = II // 128
SCALE_A = 1.0 / math.sqrt(128.0)
SCALE_B = float(1.0 / np.sqrt(np.float32(128.0)))
NEG = -1.0e9
EPS = 1e-5

ENGS = ("tensor", "vector", "scalar", "gpsimd", "sync")
SBUF_BASE = 16512
SBUF_END = 229344


class W:

    serialize_dma = True

    def __init__(self, nc, es):
        self.nc = nc
        self.es = es
        self.steps = []
        self.sems = {}
        self.cnt = {}
        self.hwm = {e: {} for e in ENGS}

    def sem(self, name):
        self.sems[name] = self.es.enter_context(self.nc.semaphore(name))
        self.cnt[name] = 0

    def wait(self, eng, *toks, force=False):
        need = {}
        for t in toks:
            if not t:
                continue
            s, v, peng = t
            if peng == eng and not force:
                continue
            if self.hwm[eng].get(s, 0) >= v:
                continue
            need[s] = max(need.get(s, 0), v)
        for s, v in sorted(need.items()):
            self.hwm[eng][s] = v
            h = self.sems[s]
            self.steps.append((eng, lambda e, h=h, v=v: e.wait_ge(h, v)))

    def op(self, eng, fn, sem=None, k=1):
        if sem is None:
            self.steps.append((eng, fn))
            return None
        h = self.sems[sem]
        self.steps.append((eng, lambda e, fn=fn, h=h, k=k: fn(e).then_inc(h, k)))
        self.cnt[sem] += k
        return (sem, self.cnt[sem], eng)

    def petok(self):
        return ("pe", self.cnt["pe"], "tensor")

    def dma(self, qeng, out, in_, odd=False, **kw):
        sem = "d_odd" if odd else ("d_" + qeng)
        h = self.sems[sem]

        def fn(e, out=out, in_=in_, kw=kw, h=h):
            e.dma_start(out=out, in_=in_, **kw).then_inc(h, 16)

        self.steps.append((qeng, fn))
        self.cnt[sem] += 16
        tok = (sem, self.cnt[sem], None)
        if odd or W.serialize_dma:
            self.wait(qeng, tok, force=True)
        return tok

    def barrier(self):
        for eng in ENGS:
            self.op(eng, lambda e: e.nop(), sem="bar")
        bar_tok = ("bar", self.cnt["bar"], None)
        dtoks = [("d_sync", self.cnt["d_sync"], None),
                 ("d_gpsimd", self.cnt["d_gpsimd"], None)]
        for eng in ENGS:
            self.wait(eng, bar_tok, *dtoks)

    def flush(self, block):
        per = {e: [] for e in ENGS}
        for eng, fn in self.steps:
            per[eng].append(fn)
        for eng in ENGS:
            fns = per[eng]

            def body(e, fns=fns):
                for fn in fns:
                    fn(e)

            getattr(block, eng)(body)


def build_program(nsub=4, nheads=16, dump=(), stop_after=None, b_stage=None, f_stage=None):
    nc = bass.Bass("TRN2", target_bir_lowering=False)

    x_in = nc.dram_tensor("x", [S, D], F32, kind="ExternalInput").ap()
    a_wq = nc.dram_tensor("a_wq", [NH, 128, D], F32, kind="ExternalInput").ap()
    a_wk = nc.dram_tensor("a_wk", [NH, 128, D], F32, kind="ExternalInput").ap()
    a_wv = nc.dram_tensor("a_wv", [NH, 128, D], F32, kind="ExternalInput").ap()
    a_wo = nc.dram_tensor("a_wo", [NH, D, 128], F32, kind="ExternalInput").ap()
    b_wq = nc.dram_tensor("b_wq", [NH, D, 128], F32, kind="ExternalInput").ap()
    b_wk = nc.dram_tensor("b_wk", [NH, D, 128], F32, kind="ExternalInput").ap()
    b_wv = nc.dram_tensor("b_wv", [NH, D, 128], F32, kind="ExternalInput").ap()
    b_wo = nc.dram_tensor("b_wo", [NH, 128, D], F32, kind="ExternalInput").ap()
    b_bq = nc.dram_tensor("b_bq", [NH, 128], F32, kind="ExternalInput").ap()
    b_bk = nc.dram_tensor("b_bk", [NH, 128], F32, kind="ExternalInput").ap()
    b_bv = nc.dram_tensor("b_bv", [NH, 128], F32, kind="ExternalInput").ap()
    b_bo = nc.dram_tensor("b_bo", [D], F32, kind="ExternalInput").ap()
    g_w = nc.dram_tensor("g_w", [II, D], I32, kind="ExternalInput").ap()
    g_s = nc.dram_tensor("g_s", [II], F32, kind="ExternalInput").ap()
    u_w = nc.dram_tensor("u_w", [II, D], I32, kind="ExternalInput").ap()
    u_s = nc.dram_tensor("u_s", [II], F32, kind="ExternalInput").ap()
    d_w = nc.dram_tensor("d_w", [D, II], I32, kind="ExternalInput").ap()
    d_s = nc.dram_tensor("d_s", [D], F32, kind="ExternalInput").ap()
    y = nc.dram_tensor("y", [S, D], F32, kind="ExternalOutput").ap()
    dmp = {}
    for nm, shape in (("lnf", [128, NDC * S]), ("qb", [128, S]), ("kb", [128, S]),
                      ("vt", [128, S]), ("vf", [128, S]), ("pb", [128, S]),
                      ("pt", [128, NST * 256]), ("z", [128, 4 * S]),
                      ("wqt", [128, NDC * 128]), ("wot", [128, 4 * D])):
        if nm in dump:
            dmp[nm] = nc.dram_tensor("d_" + nm, shape, BF16, kind="ExternalOutput").ap()
    if "stats" in dump:
        dmp["stats"] = nc.dram_tensor("d_stats", [128, 64], F32, kind="ExternalOutput").ap()
    if "wot2" in dump:
        dmp["wot2"] = nc.dram_tensor("d_wot2", [128, 4 * D], BF16, kind="ExternalOutput").ap()
    if "hc" in dump:
        dmp["hc"] = nc.dram_tensor("d_hc", [NIC, 128, S], BF16, kind="ExternalOutput").ap()
    if "wdc" in dump:
        dmp["wdc"] = nc.dram_tensor("d_wdc", [NDC, NIC, 128, 128], BF16, kind="ExternalOutput").ap()
    if "flnf" in dump:
        dmp["flnf"] = nc.dram_tensor("d_flnf", [128, NDC * S], BF16, kind="ExternalOutput").ap()
    if "wgw" in dump:
        dmp["wgw"] = nc.dram_tensor("d_wgw", [128, D], BF16, kind="ExternalOutput").ap()
        dmp["wuw"] = nc.dram_tensor("d_wuw", [128, D], BF16, kind="ExternalOutput").ap()

    x_state = nc.dram_tensor("x_state", [S, D], F32, kind="Internal").ap()
    wdT_c = nc.dram_tensor("wdT_c", [NDC, NIC, 128, 128], BF16, kind="Internal").ap()
    h_c = nc.dram_tensor("h_c", [NIC, 128, S], BF16, kind="Internal").ap()

    ident_c = nc.inline_tensor(np.eye(128, dtype=BF), name="identc").ap()
    idf_c = nc.inline_tensor(np.eye(128, dtype=np.float32), name="idfc").ap()
    mask_np = np.where(np.arange(128)[None, :] > np.arange(128)[:, None],
                       np.float32(NEG), np.float32(0.0)).astype(BF)
    mask_c = nc.inline_tensor(mask_np, name="maskc").ap()
    ones_c = nc.inline_tensor(np.ones((1, 128), dtype=BF), name="onesc").ap()

    BB = SBUF_BASE

    def sb(name, shape, dtype, off):
        end = BB + off + shape[-1] * (2 if dtype == BF16 else 4) * (1 if len(shape) == 2 else shape[1])
        assert end <= SBUF_END, (name, off, end)
        return nc.alloc_sbuf_tensor_at(name, shape, dtype, offset=BB + off)

    XLN = [sb(f"XLN{i}", [128, D], F32, i * 8192) for i in range(2)]
    XRMW = [sb(f"XRMW{i}", [128, 512], F32, 16384 + i * 2048) for i in range(4)]
    IDENT = sb("IDENT", [128, 128], BF16, 65536)
    MASKA = sb("MASKA", [128, 128], BF16, 65792)
    ONESC = sb("ONESC", [1, 128], BF16, 66048)
    IDF32 = sb("IDF32", [128, 128], F32, 66304)
    ST_NM = sb("ST_NM", [128, 16], F32, 66816)
    ST_SQ = sb("ST_SQ", [128, 16], F32, 66880)
    ST_RS = sb("ST_RS", [128, 16], F32, 66944)
    ST_NR = sb("ST_NR", [128, 16], F32, 67008)
    ST_T1 = sb("ST_T1", [128, 16], F32, 67072)
    SMAX = sb("SMAX", [128, 16], F32, 67136)
    SSUM = sb("SSUM", [128, 16], F32, 67200)
    SRL = sb("SRL", [128, 16], F32, 67264)
    SCR = sb("SCR", [128, 8], F32, 67328)
    A0 = 67584

    LNF = sb("LNF", [128, NDC * S], BF16, A0)
    Z = sb("Z", [128, 4 * S], BF16, A0 + 65536)
    WOT = sb("WOT", [128, 4 * D], BF16, A0 + 81920)
    QB = sb("QB", [128, S], BF16, A0 + 98304)
    KB = sb("KB", [128, S], BF16, A0 + 102400)
    VT = sb("VT", [128, S], BF16, A0 + 106496)
    VF = sb("VF", [128, S], BF16, A0 + 110592)
    PB = sb("PB", [128, S], BF16, A0 + 114688)
    PT = sb("PT", [128, NST * 256], BF16, A0 + 118784)
    WQT = sb("WQT", [128, NDC * 128], BF16, A0 + 126976)
    WKT = sb("WKT", [128, NDC * 128], BF16, A0 + 131072)
    WVT = sb("WVT", [128, NDC * 128], BF16, A0 + 135168)
    RAWA = sb("RAWA", [128, NDC * 128], BF16, A0 + 139264)
    BO_ROW = sb("BO_ROW", [1, D], BF16, A0 + 139264)
    BQT = sb("BQT", [128, 16], F32, A0 + 143360)
    BKT = sb("BKT", [128, 16], F32, A0 + 143424)
    BVT = sb("BVT", [128, 16], F32, A0 + 143488)
    TMPB = sb("TMPB", [64, 128], F32, A0 + 143552)

    RAWG = [sb(f"RAWG{i}", [128, D], I32, A0 + 65536 + i * 8192) for i in range(2)]
    DQ = [sb(f"DQ{i}", [128, D], BF16, A0 + 81920 + i * 4096) for i in range(3)]
    WGW = [sb(f"WGW{i}", [128, D], BF16, A0 + 94208 + i * 4096) for i in range(2)]
    WUW = [sb(f"WUW{i}", [128, D], BF16, A0 + 102400 + i * 4096) for i in range(2)]
    WDW = [sb(f"WDW{i}", [128, D], BF16, A0 + 110592 + i * 4096) for i in range(2)]
    GSB = [sb(f"GSB{i}", [128, 512], F32, A0 + 118784 + i * 2048) for i in range(2)]
    HST = [sb(f"HST{i}", [128, S], BF16, A0 + 122880 + i * 4096) for i in range(2)]
    RAWD = sb("RAWD", [128, D], I32, A0 + 131072)
    GSC = sb("GSC", [128, 64], F32, A0 + 139264)
    USC = sb("USC", [128, 64], F32, A0 + 139520)
    DSC = sb("DSC", [128, 16], F32, A0 + 139776)
    TMPS = sb("TMPS", [64, 128], F32, A0 + 139840)

    HBLK = sb("HBLK", [128, NIC * 1024], BF16, A0)
    WDP = [sb(f"WDP{i}", [128, 512], BF16, A0 + 131072 + i * 1024) for i in range(4)]

    ps = nc.alloc_psum_tensor("ps", [128, 4096], F32)
    psb = ps.bitcast(BF16)
    TR_OFF = 4096

    es = contextlib.ExitStack()
    with es:
        w = W(nc, es)
        for s_ in ("pe", "ve", "ac", "bar", "d_sync", "d_gpsimd", "d_odd"):
            w.sem(s_)

        w.dma("gpsimd", IDENT[:], ident_c)
        w.dma("gpsimd", IDF32[:], idf_c)
        w.dma("gpsimd", MASKA[:], mask_c)
        w.dma("gpsimd", ONESC[:], ones_c, odd=True)
        const_tok = ("d_gpsimd", w.cnt["d_gpsimd"], None)
        xtok = [[] for _ in range(NST)]
        for eng in ("tensor", "vector", "scalar"):
            w.wait(eng, const_tok)
        sub_i = [0]
        def xsrc_ap():
            return x_in if sub_i[0] == 0 else x_state
        def xdst_ap():
            return x_state

        def emit_ln(stage_bufs, ln_stage=None):
            pe_now = ("pe", w.cnt["pe"], "tensor")
            ve_now = ("ve", w.cnt["ve"], None)
            w.wait("scalar", pe_now, ve_now)
            w.wait("vector", pe_now)
            xsrc = xsrc_ap()
            sq_tok = None
            xfree = [None, None]
            for st in range(NST):
                pp = st % 2
                w.wait("gpsimd", *xtok[st],
                       *(xfree[pp] if isinstance(xfree[pp], tuple) and
                         isinstance(xfree[pp][0], tuple) else (xfree[pp],)))
                ldx = w.dma("gpsimd", XLN[pp][:], xsrc[st * 128:(st + 1) * 128, :])
                xs = XLN[pp][:]
                w.wait("vector", ldx)
                rtok = w.op("vector", lambda e, xs=xs, st=st: e.reduce_sum(
                    out=ST_NM[:, st:st + 1], in_=xs, axis=AX.X, negate=True),
                    sem="ve")
                if st == NST - 1:
                    w.wait("vector", rtok, force=True)
                w.wait("scalar", ldx)
                sq_tok = w.op("scalar", lambda e, xs=xs, st=st: e.activation(
                    out=ps[:, 0:2048], in_=xs, func=AF.Square,
                    accum_out=ST_SQ[:, st:st + 1]), sem="ac")
                xfree[pp] = (rtok, sq_tok)
            if ln_stage in (10, 13):
                return None
            w.op("vector", lambda e: e.tensor_scalar(
                out=ST_NM[:], in0=ST_NM[:], scalar1=1.0 / D, scalar2=None, op0=ALU.mult))
            w.wait("vector", sq_tok)
            w.op("vector", lambda e: e.tensor_scalar(
                out=ST_T1[:], in0=ST_SQ[:], scalar1=1.0 / D, scalar2=None, op0=ALU.mult))
            w.op("vector", lambda e: e.tensor_tensor(
                out=ST_RS[:], in0=ST_NM[:], in1=ST_NM[:], op=ALU.mult))
            w.op("vector", lambda e: e.tensor_tensor(
                out=ST_T1[:], in0=ST_T1[:], in1=ST_RS[:], op=ALU.subtract))
            var_tok = w.op("vector", lambda e: e.tensor_scalar(
                out=ST_T1[:], in0=ST_T1[:], scalar1=EPS, scalar2=None, op0=ALU.add), sem="ve")
            w.wait("scalar", var_tok)
            std_tok = w.op("scalar", lambda e: e.activation(
                out=ST_T1[:], in_=ST_T1[:], func=AF.Sqrt), sem="ac")
            w.wait("vector", std_tok)
            rc_tok = w.op("vector", lambda e: e.reciprocal(
                out=ST_RS[:], in_=ST_T1[:]), sem="ve")
            w.wait("vector", rc_tok, force=True)
            rs_tok = w.op("vector", lambda e: e.tensor_tensor(
                out=ST_NR[:], in0=ST_NM[:], in1=ST_RS[:], op=ALU.mult), sem="ve")

            if ln_stage == 11:
                return None
            tr_free = [None, None]
            stage_free = [None, None]
            last = None
            for st in range(NST):
                pp = st % 2
                w.wait("gpsimd",
                       *(xfree[pp] if isinstance(xfree[pp], tuple) and
                         isinstance(xfree[pp][0], tuple) else (xfree[pp],)))
                ldx = w.dma("gpsimd", XLN[pp][:], xsrc[st * 128:(st + 1) * 128, :])
                stg = stage_bufs[st % 2]
                w.wait("scalar", rs_tok, stage_free[st % 2], ldx)
                ntok = w.op("scalar", lambda e, stg=stg, pp=pp, st=st: e.activation(
                    out=stg[:, 0:D], in_=XLN[pp][:], func=AF.Identity,
                    scale=ST_RS[:, st:st + 1], bias=ST_NR[:, st:st + 1]), sem="ac")
                xfree[pp] = ntok
                if ln_stage == 12:
                    continue
                for half in range(2):
                    po = 4096 + half * 1024
                    w.wait("tensor", ntok, tr_free[half])
                    for j in range(8):
                        dc = half * 8 + j
                        w.op("tensor", lambda e, stg=stg, dc=dc, po=po, j=j: e.transpose(
                            out=psb[:, po + j * 128:po + (j + 1) * 128],
                            in_=stg[:, dc * 128:(dc + 1) * 128], identity=IDENT[:]),
                            sem="pe" if j == 7 else None)
                    ttok = w.petok()
                    dst = LNF.ap().rearrange("r (dc s) -> r dc s", dc=NDC)[
                        :, half * 8:(half + 1) * 8, st * 128:(st + 1) * 128]
                    src = psb[:, po:po + 1024].rearrange("r (t c) -> r t c", t=8)
                    w.wait("scalar", ttok)
                    etok = w.op("scalar", lambda e, dst=dst, src=src: e.copy(
                        out=dst, in_=src), sem="ac")
                    tr_free[half] = etok
                    if half == 1:
                        stage_free[st % 2] = w.petok()
                        last = etok
            return last

        def emit_attn(variant, stage=None):
            if stage == -1:
                return
            sc = ps[:, 0:2048]
            zb = [ps[:, 2560:2816], ps[:, 2816:3072]]
            pj = [ps[:, 3072:3584], ps[:, 3584:4096]]
            scale = SCALE_A if variant == "a" else SCALE_B
            srcs = ((a_wq, a_wk, a_wv, a_wo) if variant == "a"
                    else (b_wq, b_wk, b_wv, b_wo))

            ln_done = emit_ln((PB, QB), ln_stage=stage if stage in (10, 11, 12, 13) else None)
            if stage in (0, 10, 11, 12, 13):
                return

            bias_last = None
            if variant == "b":
                pb_free = None
                prev_btk = None
                for bsrc, bdst in ((b_bq, BQT), (b_bk, BKT), (b_bv, BVT)):
                    w.wait("gpsimd", prev_btk)
                    t0 = w.dma("gpsimd", TMPB[0:16, :], bsrc, odd=True)
                    w.wait("tensor", t0, pb_free, ln_done)
                    tk = w.op("tensor", lambda e: e.transpose(
                        out=ps[:, 2560:2576], in_=TMPB[0:16, :], identity=IDF32[0:16, 0:16]),
                        sem="pe")
                    prev_btk = tk
                    w.wait("vector", tk)
                    bias_last = w.op("vector", lambda e, bdst=bdst: e.tensor_copy(
                        out=bdst[:], in_=ps[:, 2560:2576]), sem="ve")
                    pb_free = bias_last
                bo_tok = w.dma("gpsimd", BO_ROW[:], b_bo[None, :], odd=True)
            if stage == 1:
                return

            state = {
                "q_free": None, "k_free": None, "vt_free": None, "vf_free": None,
                "wqt_free": None, "wkt_free": None, "wvt_free": None,
                "tr_free": ln_done, "prev_exp": None, "prev_ptt": None,
                "pt_read": None, "outproj_done": None,
                "pj_free": [None, None], "raw_free": None,
                "z_free": [bias_last, bias_last], "pjp": 0,
            }
            zev_last = [None]

            def prep_head(h):
                if "wot2" in dmp and h == min(NH, nheads) - 1:
                    w.wait("gpsimd", state["tr_free"], w.petok())
                    w.dma("gpsimd", dmp["wot2"], WOT[:])
                if variant == "a":
                    res = {}
                    raw_free = state["raw_free"]
                    for name, src in (("q", srcs[0]), ("k", srcs[1]),
                                      ("v", srcs[2]), ("o", srcs[3])):
                        if name == "o":
                            in_ap = src[h].rearrange("(dc r) e -> r dc e", dc=NDC)
                        else:
                            in_ap = src[h]
                        w.wait("gpsimd", raw_free)
                        ld = w.dma("gpsimd", RAWA.ap().rearrange(
                            "r (dc e) -> r dc e", dc=NDC) if name == "o" else RAWA[:], in_ap)
                        busy = {"q": state["wqt_free"], "k": state["wkt_free"],
                                "v": state["wvt_free"], "o": state["outproj_done"]}[name]
                        etok = None
                        for half in range(2):
                            w.wait("tensor", ld, state["tr_free"])
                            for j in range(8):
                                dc = half * 8 + j
                                w.op("tensor", lambda e, dc=dc, j=j: e.transpose(
                                    out=psb[:, TR_OFF + j * 128:TR_OFF + (j + 1) * 128],
                                    in_=RAWA[:, dc * 128:(dc + 1) * 128], identity=IDENT[:]),
                                    sem="pe" if j == 7 else None)
                            ttok = w.petok()
                            if name == "o":
                                dptr = WOT[:, (h % 4) * D + half * 1024:
                                           (h % 4) * D + (half + 1) * 1024]
                            else:
                                dbuf = {"q": WQT, "k": WKT, "v": WVT}[name]
                                dptr = dbuf[:, half * 1024:(half + 1) * 1024]
                            w.wait("scalar", ttok, busy if half == 0 else None)
                            etok = w.op("scalar", lambda e, dptr=dptr: e.copy(
                                out=dptr, in_=psb[:, TR_OFF:TR_OFF + 1024]), sem="ac")
                            state["tr_free"] = etok
                        raw_free = w.petok()
                        res[name] = etok
                    state["raw_free"] = raw_free
                    return res["q"], res["k"], res["v"], res["o"]
                else:
                    w.wait("gpsimd", state["wqt_free"], state["wkt_free"],
                           state["wvt_free"],
                           state["outproj_done"] if h % 4 == 0 else None)
                    rq = w.dma("gpsimd", WQT.ap().rearrange("r (dc e) -> r dc e", dc=NDC),
                               srcs[0][h].rearrange("(dc r) e -> r dc e", dc=NDC))
                    rk = w.dma("gpsimd", WKT.ap().rearrange("r (dc e) -> r dc e", dc=NDC),
                               srcs[1][h].rearrange("(dc r) e -> r dc e", dc=NDC))
                    rv = w.dma("gpsimd", WVT.ap().rearrange("r (dc e) -> r dc e", dc=NDC),
                               srcs[2][h].rearrange("(dc r) e -> r dc e", dc=NDC))
                    ro = w.dma("gpsimd", WOT[:, (h % 4) * D:(h % 4 + 1) * D], srcs[3][h])
                    return rq, rk, rv, ro

            def proj(dst, wbuf, wtok, bias_ap, dst_free):
                toks = []
                for sbi in range(4):
                    bank = pj[state["pjp"]]
                    bfree = state["pj_free"][state["pjp"]]
                    w.wait("tensor", wtok, bfree, ln_done)
                    for dc in range(NDC):
                        w.op("tensor", lambda e, bank=bank, dc=dc, sbi=sbi, wbuf=wbuf: e.matmul(
                            out=bank, lhsT=wbuf[:, dc * 128:(dc + 1) * 128],
                            rhs=LNF[:, dc * S + sbi * 512:dc * S + (sbi + 1) * 512],
                            start=(dc == 0), stop=(dc == NDC - 1)),
                            sem="pe" if dc == NDC - 1 else None)
                    mtok = w.petok()
                    w.wait("scalar", mtok, dst_free if sbi == 0 else None)
                    if bias_ap is None:
                        ev = w.op("scalar", lambda e, dst=dst, sbi=sbi, bank=bank: e.copy(
                            out=dst[:, sbi * 512:(sbi + 1) * 512], in_=bank), sem="ac")
                    else:
                        ev = w.op("scalar", lambda e, dst=dst, sbi=sbi, bank=bank,
                                  bias_ap=bias_ap: e.activation(
                                      out=dst[:, sbi * 512:(sbi + 1) * 512], in_=bank,
                                      func=AF.Identity, bias=bias_ap), sem="ac")
                    state["pj_free"][state["pjp"]] = ev
                    state["pjp"] ^= 1
                    toks.append(ev)
                return toks

            rmw_i = [0]
            rmw_free = [None, None, None, None]
            gran_first = [[True] * 4 for _ in range(NST)]
            wo_tok = None
            for h in range(min(NH, nheads)):
                if stop_after == "noprep" and h == min(NH, nheads) - 1:
                    break
                wq_tok, wk_tok, wv_tok, wo_tok_h = prep_head(h)
                wo_tok = wo_tok_h
                if stage == 2 and h == 0:
                    break
                if stop_after == "prep" and h == min(NH, nheads) - 1:
                    break
                if variant == "b" and h == 0:
                    w.wait("scalar", bias_last)
                    w.wait("tensor", bo_tok)

                bq_ap = BQT[:, h:h + 1] if variant == "b" else None
                bk_ap = BKT[:, h:h + 1] if variant == "b" else None
                bv_ap = BVT[:, h:h + 1] if variant == "b" else None
                q_toks = proj(QB, WQT, wq_tok, bq_ap, state["q_free"])
                state["wqt_free"] = w.petok()
                k_toks = proj(KB, WKT, wk_tok, bk_ap, state["k_free"])
                state["wkt_free"] = w.petok()
                vf_toks = proj(VF, WVT, wv_tok, bv_ap, state["vf_free"])
                state["wvt_free"] = w.petok()
                if stage == 3 and h == 0:
                    break
                if stop_after == "qkv" and h == min(NH, nheads) - 1:
                    break

                vt_toks = []
                for half in range(2):
                    w.wait("tensor", vf_toks[2 * half + 1], state["tr_free"],
                           state["vt_free"] if half == 0 else None)
                    for j in range(8):
                        stl = half * 8 + j
                        w.op("tensor", lambda e, stl=stl, j=j: e.transpose(
                            out=psb[:, TR_OFF + j * 128:TR_OFF + (j + 1) * 128],
                            in_=VF[:, stl * 128:(stl + 1) * 128], identity=IDENT[:]),
                            sem="pe" if j == 7 else None)
                    ttok = w.petok()
                    w.wait("scalar", ttok)
                    ev = w.op("scalar", lambda e, half=half: e.copy(
                        out=VT[:, half * 1024:(half + 1) * 1024],
                        in_=psb[:, TR_OFF:TR_OFF + 1024]), sem="ac")
                    state["tr_free"] = ev
                    vt_toks.append(ev)
                state["vf_free"] = w.petok()

                for qt in range(NST):
                    kq = (qt + 1) * 128
                    nch = (kq + 511) // 512
                    w.wait("tensor", state["prev_exp"], q_toks[qt // 4],
                           k_toks[min(3, (kq - 1) // 512)])
                    for j in range(nch):
                        n = min(512, kq - j * 512)
                        w.op("tensor", lambda e, j=j, n=n, qt=qt: e.matmul(
                            out=sc[:, j * 512:j * 512 + n],
                            lhsT=QB[:, qt * 128:(qt + 1) * 128],
                            rhs=KB[:, j * 512:j * 512 + n], start=True, stop=True),
                            sem="pe" if j == nch - 1 else None)
                    sc_tok = w.petok()
                    w.wait("vector", sc_tok)
                    w.op("vector", lambda e, kq=kq: e.tensor_tensor(
                        out=sc[:, kq - 128:kq], in0=sc[:, kq - 128:kq],
                        in1=MASKA[:], op=ALU.add))
                    rm_tok = w.op("vector", lambda e, kq=kq, qt=qt: e.reduce_max(
                        out=SMAX[:, qt:qt + 1], in_=sc[:, 0:kq], axis=AX.X, negate=True),
                        sem="ve")
                    w.wait("vector", rm_tok, force=True)
                    mx_tok = w.op("vector", lambda e, qt=qt: e.tensor_scalar(
                        out=SMAX[:, qt:qt + 1], in0=SMAX[:, qt:qt + 1], scalar1=scale,
                        scalar2=None, op0=ALU.mult), sem="ve")
                    w.wait("scalar", mx_tok, state["prev_ptt"])
                    exp_tok = w.op("scalar", lambda e, kq=kq, qt=qt: e.activation(
                        out=PB[:, 0:kq], in_=sc[:, 0:kq], func=AF.Exp, scale=scale,
                        bias=SMAX[:, qt:qt + 1], accum_out=SSUM[:, qt:qt + 1]), sem="ac")
                    state["prev_exp"] = exp_tok
                    w.wait("vector", exp_tok)
                    rl_tok = w.op("vector", lambda e, qt=qt: e.reciprocal(
                        out=SRL[:, qt:qt + 1], in_=SSUM[:, qt:qt + 1]), sem="ve")
                    w.wait("scalar", rl_tok)
                    norm_tok = w.op("scalar", lambda e, kq=kq, qt=qt: e.activation(
                        out=PB[:, 0:kq], in_=PB[:, 0:kq], func=AF.Copy,
                        scale=SRL[:, qt:qt + 1]), sem="ac")

                    qcol = (qt % 2) * 128
                    for base in range(0, qt + 1, 8):
                        cnt = min(8, qt + 1 - base)
                        w.wait("tensor", norm_tok, state["tr_free"],
                               state["pt_read"] if base == 0 else None)
                        for j in range(cnt):
                            kb = base + j
                            w.op("tensor", lambda e, kb=kb, j=j: e.transpose(
                                out=psb[:, TR_OFF + j * 128:TR_OFF + (j + 1) * 128],
                                in_=PB[:, kb * 128:(kb + 1) * 128], identity=IDENT[:]),
                                sem="pe" if j == cnt - 1 else None)
                        ttok = w.petok()
                        dst = PT.ap().rearrange("r (kb q) -> r kb q", kb=NST)[
                            :, base:base + cnt, qcol:qcol + 128]
                        src = psb[:, TR_OFF:TR_OFF + cnt * 128].rearrange(
                            "r (t c) -> r t c", t=cnt)
                        w.wait("scalar", ttok)
                        ev = w.op("scalar", lambda e, dst=dst, src=src: e.copy(
                            out=dst, in_=src), sem="ac")
                        state["tr_free"] = ev
                    state["prev_ptt"] = w.petok()
                    pt_ev_tok = state["tr_free"]

                    if qt % 2 == 1:
                        g = qt // 2
                        w.wait("vector", state["pt_read"])
                        mz_tok = w.op("vector", lambda e, qt=qt: e.memset(
                            PT.ap().rearrange("r (kb q) -> r kb q", kb=NST)[
                                :, qt:qt + 1, 0:128], 0.0), sem="ve")
                        zr = zb[g % 2]
                        w.wait("tensor", pt_ev_tok, mz_tok, state["z_free"][g % 2],
                               *vt_toks)
                        for kb in range(qt + 1):
                            w.op("tensor", lambda e, kb=kb, zr=zr, qt=qt: e.matmul(
                                out=zr, lhsT=VT[:, kb * 128:(kb + 1) * 128],
                                rhs=PT[:, kb * 256:(kb + 1) * 256],
                                start=(kb == 0), stop=(kb == qt)),
                                sem="pe" if kb == qt else None)
                        z_tok = w.petok()
                        state["pt_read"] = z_tok
                        w.wait("scalar", z_tok)
                        zev_tok = w.op("scalar", lambda e, h=h, g=g, zr=zr: e.copy(
                            out=Z[:, (h % 4) * S + g * 256:(h % 4) * S + (g + 1) * 256],
                            in_=zr), sem="ac")
                        state["z_free"][g % 2] = zev_tok
                        zev_last[0] = zev_tok

                state["vt_free"] = w.petok()
                state["q_free"] = w.petok()
                state["k_free"] = w.petok()

                if h % 4 == 3:
                    for st in range(NST):
                        for db in range(4):
                            bank = pj[state["pjp"]]
                            bfree = state["pj_free"][state["pjp"]]
                            w.wait("tensor", bfree,
                                   zev_last[0] if (st == 0 and db == 0) else None,
                                   wo_tok if (st == 0 and db == 0) else None)
                            for hh in range(4):
                                is_last = (hh == 3 and variant == "a")
                                w.op("tensor", lambda e, hh=hh, st=st, db=db, bank=bank:
                                     e.matmul(
                                         out=bank,
                                         lhsT=Z[:, hh * S + st * 128:hh * S + (st + 1) * 128],
                                         rhs=WOT[:, hh * D + db * 512:hh * D + (db + 1) * 512],
                                         start=(hh == 0),
                                         stop=(hh == 3 and variant == "a")),
                                     sem="pe" if is_last else None)
                            if variant == "b":
                                w.op("tensor", lambda e, db=db, bank=bank: e.matmul(
                                    out=bank, lhsT=ONESC[0:1, 0:128],
                                    rhs=BO_ROW[0:1, db * 512:(db + 1) * 512],
                                    start=False, stop=True), sem="pe")
                            mtok = w.petok()
                            rp = rmw_i[0] % 4
                            rmw_i[0] += 1
                            w.wait("sync", rmw_free[rp], *xtok[st])
                            rsrc = xsrc_ap() if gran_first[st][db] else x_state
                            gran_first[st][db] = False
                            ldr = w.dma("sync", XRMW[rp][:],
                                        rsrc[st * 128:(st + 1) * 128,
                                             db * 512:(db + 1) * 512])
                            w.wait("vector", mtok, ldr)
                            w.op("vector", lambda e, rp=rp, bank=bank:
                                 e.tensor_tensor(
                                     out=XRMW[rp][:], in0=XRMW[rp][:],
                                     in1=bank, op=ALU.add))
                            ad = w.op("vector", lambda e, rp=rp: e.tensor_copy(
                                out=SCR[:, 0:1], in_=XRMW[rp][:, 511:512]), sem="ve")
                            w.wait("sync", ad)
                            sto = w.dma("sync", x_state[st * 128:(st + 1) * 128,
                                                          db * 512:(db + 1) * 512],
                                        XRMW[rp][:])
                            w.wait("sync", sto)
                            if len(xtok[st]) == 4 or xtok[st] and xtok[st][0][0] != "d_gpsimd":
                                xtok[st] = []
                            xtok[st].append(sto)
                            rmw_free[rp] = sto
                            state["pj_free"][state["pjp"]] = ad
                            state["pjp"] ^= 1
                    state["outproj_done"] = w.petok()

            if dmp:
                w.barrier()
                for nm, buf in (("lnf", LNF), ("qb", QB), ("kb", KB), ("vt", VT),
                                ("vf", VF), ("pb", PB), ("pt", PT), ("z", Z),
                                ("wqt", WQT), ("wot", WOT)):
                    if nm in dmp:
                        w.dma("gpsimd", dmp[nm], buf[:])
                if "stats" in dmp:
                    for i, sbuf_ in enumerate((ST_NM, ST_SQ, ST_RS, ST_NR)):
                        w.dma("gpsimd", dmp["stats"][:, i*16:(i+1)*16], sbuf_[:])
                w.wait("gpsimd", ("d_gpsimd", w.cnt["d_gpsimd"], None))
                w.barrier()

        def emit_ffn():
            gband = [ps[:, 0:512], ps[:, 512:1024]]
            uband = [ps[:, 1024:1536], ps[:, 1536:2048]]

            sc_done = []
            pb_free = None
            prev_tk = None
            for srcv, dst, n in ((g_s, GSC, 64), (u_s, USC, 64), (d_s, DSC, 16)):
                w.wait("gpsimd", prev_tk)
                ld = w.dma("gpsimd", TMPS[0:n, :], srcv.rearrange("(a r) -> a r", r=128), odd=True)
                w.wait("tensor", ld, pb_free)
                tk = w.op("tensor", lambda e, n=n: e.transpose(
                    out=ps[:, 3584:3584 + n], in_=TMPS[0:n, :],
                    identity=IDF32[0:n, 0:n]), sem="pe")
                prev_tk = tk
                w.wait("vector", tk)
                ct = w.op("vector", lambda e, dst=dst, n=n: e.tensor_copy(
                    out=dst[:, 0:n], in_=ps[:, 3584:3584 + n]), sem="ve")
                sc_done.append(ct)
                pb_free = ct

            ln_done = emit_ln((HST[0], HST[1]))

            raw_free = [None, None, None]
            dq_free = [None, None, None]
            wg_busy = [None, None]
            wu_busy = [None, None]
            wd_busy = [None, None]
            hst_free = [None, None]
            gs_free = [None, None]
            band_free = [None, None, None, None]
            trf = [ln_done]
            hc_last = None
            wd_last = None

            def wtranspose(dqbuf, dq_tok, dst, dst_busy):
                out_tok = None
                for half in range(2):
                    w.wait("tensor", dq_tok, trf[0])
                    for j in range(8):
                        dc = half * 8 + j
                        w.op("tensor", lambda e, dqbuf=dqbuf, dc=dc, j=j: e.transpose(
                            out=psb[:, TR_OFF + j * 128:TR_OFF + (j + 1) * 128],
                            in_=dqbuf[:, dc * 128:(dc + 1) * 128], identity=IDENT[:]),
                            sem="pe" if j == 7 else None)
                    ttok = w.petok()
                    w.wait("scalar", ttok, dst_busy if half == 0 else None)
                    out_tok = w.op("scalar", lambda e, dst=dst, half=half: e.copy(
                        out=dst[:, half * 1024:(half + 1) * 1024],
                        in_=psb[:, TR_OFF:TR_OFF + 1024]), sem="ac")
                    trf[0] = out_tok
                return out_tok, w.petok()

            for ic in range(NIC):
                pp = ic % 2
                hpp = ic % 2
                w.wait("sync", raw_free[0])
                ldg = w.dma("sync", RAWG[0][:], g_w[ic * 128:(ic + 1) * 128, :])
                w.wait("vector", ldg, dq_free[0], sc_done[0])
                dqg = w.op("vector", lambda e, ic=ic: e.tensor_scalar(
                    out=DQ[0][:], in0=RAWG[0][:], scalar1=-128.0,
                    scalar2=GSC[:, ic:ic + 1], op0=ALU.add, op1=ALU.mult), sem="ve")
                raw_free[0] = dqg
                wg_tok, dqf = wtranspose(DQ[0], dqg, WGW[pp], wg_busy[pp])
                dq_free[0] = dqf
                w.wait("sync", raw_free[1])
                ldu = w.dma("sync", RAWG[1][:], u_w[ic * 128:(ic + 1) * 128, :])
                w.wait("vector", ldu, dq_free[1], sc_done[1])
                dqu = w.op("vector", lambda e, ic=ic: e.tensor_scalar(
                    out=DQ[1][:], in0=RAWG[1][:], scalar1=-128.0,
                    scalar2=USC[:, ic:ic + 1], op0=ALU.add, op1=ALU.mult), sem="ve")
                raw_free[1] = dqu
                wu_tok, dqf = wtranspose(DQ[1], dqu, WUW[pp], wu_busy[pp])
                dq_free[1] = dqf

                h_toks = []
                for half in range(2):
                    for which, wbuf, wtok, bands in (
                            (0, WGW[pp], wg_tok, gband), (1, WUW[pp], wu_tok, uband)):
                        w.wait("tensor", wtok, ln_done,
                               band_free[2 * which], band_free[2 * which + 1])
                        for dc in range(NDC):
                            for sb2 in range(2):
                                sbi = half * 2 + sb2
                                w.op("tensor", lambda e, bands=bands, sb2=sb2, dc=dc,
                                     sbi=sbi, wbuf=wbuf: e.matmul(
                                         out=bands[sb2],
                                         lhsT=wbuf[:, dc * 128:(dc + 1) * 128],
                                         rhs=LNF[:, dc * S + sbi * 512:dc * S + (sbi + 1) * 512],
                                         start=(dc == 0), stop=(dc == NDC - 1)),
                                     sem="pe" if (dc == NDC - 1 and sb2 == 1) else None)
                        if which == 0:
                            g_mm = w.petok()
                        else:
                            u_mm = w.petok()
                    for sb2 in range(2):
                        sbi = half * 2 + sb2
                        w.wait("scalar", g_mm, gs_free[sb2])
                        sil = w.op("scalar", lambda e, sb2=sb2: e.activation(
                            out=GSB[sb2][:], in_=gband[sb2], func=AF.Silu), sem="ac")
                        band_free[sb2] = sil
                        w.wait("vector", sil, u_mm,
                               hst_free[hpp] if (half == 0 and sb2 == 0) else None)
                        ht = w.op("vector", lambda e, sb2=sb2, sbi=sbi, hpp=hpp:
                                  e.tensor_tensor(
                                      out=HST[hpp][:, sbi * 512:(sbi + 1) * 512],
                                      in0=GSB[sb2][:], in1=uband[sb2], op=ALU.mult),
                                  sem="ve")
                        band_free[2 + sb2] = ht
                        gs_free[sb2] = ht
                        h_toks.append(ht)
                wg_busy[pp] = w.petok()
                wu_busy[pp] = w.petok()
                hfl = w.op("vector", lambda e, hpp=hpp: e.tensor_copy(
                    out=SCR[:, 1:2], in_=HST[hpp].bitcast(F32)[:, 1023:1024]), sem="ve")
                w.wait("sync", *h_toks, hfl)
                hc_last = w.dma("sync", h_c[ic], HST[hpp][:])
                hst_free[hpp] = hc_last

                dcW, icg = ic // 4, ic % 4
                w.wait("sync", raw_free[2])
                ldd = w.dma("sync", RAWD[:],
                            d_w[dcW * 128:(dcW + 1) * 128, icg * 2048:(icg + 1) * 2048])
                w.wait("vector", ldd, dq_free[2], sc_done[2])
                dqd = w.op("vector", lambda e, dcW=dcW: e.tensor_scalar(
                    out=DQ[2][:], in0=RAWD[:], scalar1=-128.0,
                    scalar2=DSC[:, dcW:dcW + 1], op0=ALU.add, op1=ALU.mult), sem="ve")
                raw_free[2] = dqd
                wdt_tok, dqf = wtranspose(DQ[2], dqd, WDW[pp], wd_busy[pp])
                dq_free[2] = dqf
                w.wait("sync", wdt_tok)
                wd_last = w.dma("sync", wdT_c[dcW, icg * 16:(icg + 1) * 16].rearrange(
                    "t r c -> r t c"), WDW[pp].ap().rearrange("r (t c) -> r t c", t=16))
                wd_busy[pp] = wd_last

            if dmp:
                w.wait("gpsimd", ("d_sync", w.cnt["d_sync"], None),
                       ("ve", w.cnt["ve"], None), ("ac", w.cnt["ac"], None),
                       w.petok())
                if "flnf" in dmp:
                    w.dma("gpsimd", dmp["flnf"], LNF[:])
                if "wgw" in dmp:
                    w.dma("gpsimd", dmp["wgw"], WGW[(NIC - 1) % 2][:])
                    w.dma("gpsimd", dmp["wuw"], WUW[(NIC - 1) % 2][:])
                if "hc" in dmp:
                    w.dma("gpsimd", dmp["hc"], h_c)
                if "wdc" in dmp:
                    w.dma("gpsimd", dmp["wdc"], wdT_c)
            w.barrier()
            if f_stage == 1:
                return

            rmw_i = [0]
            rmw_free = [None, None, None, None]
            wdp_free = [None] * 4
            for sbh in range(2):
                hb_last = None
                for ic in range(NIC):
                    hb_last = w.dma("sync", HBLK[:, ic * 1024:(ic + 1) * 1024],
                                    h_c[ic][:, sbh * 1024:(sbh + 1) * 1024])
                prev_db_add = None
                for db in range(4):
                    w.wait("tensor", ("ve", w.cnt["ve"], None),
                           ("d_sync", w.cnt["d_sync"], None))
                    mm_tok = None
                    for ic in range(NIC):
                        wb = WDP[ic % 4]
                        w.wait("sync", wdp_free[ic % 4])
                        ldw_ = w.dma("sync", wb.ap().rearrange("r (t c) -> r t c", t=4),
                                     wdT_c[4 * db:4 * db + 4, ic].rearrange("t r c -> r t c"))
                        w.wait("tensor", ldw_, hb_last if ic == 0 else None,
                               prev_db_add if ic == 0 else None)
                        for stl in range(8):
                            w.op("tensor", lambda e, stl=stl, ic=ic, wb=wb: e.matmul(
                                out=ps[:, stl * 512:(stl + 1) * 512],
                                lhsT=HBLK[:, ic * 1024 + stl * 128:ic * 1024 + (stl + 1) * 128],
                                rhs=wb[:], start=(ic == 0), stop=(ic == NIC - 1)),
                                sem="pe" if stl == 7 else None)
                        wdp_free[ic % 4] = w.petok()
                    mm_tok = w.petok()
                    for stl in range(8):
                        st = sbh * 8 + stl
                        rp = rmw_i[0] % 4
                        rmw_i[0] += 1
                        w.wait("sync", rmw_free[rp], *xtok[st])
                        ldr = w.dma("sync", XRMW[rp][:],
                                    xsrc_ap()[st * 128:(st + 1) * 128,
                                              db * 512:(db + 1) * 512])
                        w.wait("vector", mm_tok, ldr)
                        w.op("vector", lambda e, rp=rp, stl=stl: e.tensor_tensor(
                            out=XRMW[rp][:], in0=XRMW[rp][:],
                            in1=ps[:, stl * 512:(stl + 1) * 512], op=ALU.add))
                        ad = w.op("vector", lambda e, rp=rp: e.tensor_copy(
                            out=SCR[:, 0:1], in_=XRMW[rp][:, 511:512]), sem="ve")
                        w.wait("sync", ad)
                        sto = w.dma("sync", x_state[st * 128:(st + 1) * 128,
                                                        db * 512:(db + 1) * 512],
                                    XRMW[rp][:])
                        w.wait("sync", sto)
                        if len(xtok[st]) == 4 or xtok[st] and xtok[st][0][0] != "d_gpsimd":
                            xtok[st] = []
                        xtok[st].append(sto)
                        rmw_free[rp] = sto
                        prev_db_add = ad
                w.wait("sync", w.petok())

        seq = ["attn_a", "ffn", "attn_b", "ffn"][:nsub]
        for i, ph in enumerate(seq):
            sub_i[0] = i
            if ph == "attn_a":
                emit_attn("a")
            elif ph == "attn_b":
                emit_attn("b", stage=b_stage)
            else:
                emit_ffn()
                if f_stage == 1:
                    break
            w.barrier()

        if True:
            xfin = x_in if nsub == 0 else x_state
            for st in range(NST):
                ldx = w.dma("gpsimd", XLN[st % 2][:], xfin[st * 128:(st + 1) * 128, :])
                w.wait("gpsimd", ldx)
                w.dma("gpsimd", y[st * 128:(st + 1) * 128, :], XLN[st % 2][:])
        w.wait("gpsimd", ("d_gpsimd", w.cnt["d_gpsimd"], None))
        w.wait("sync", ("d_sync", w.cnt["d_sync"], None))

        block = es.enter_context(nc.Block())
        w.flush(block)
    return nc


_NMAP = {
    "x": "L0_attn_a_x",
    "a_wq": "L0_attn_a_W_Q", "a_wk": "L0_attn_a_W_K",
    "a_wv": "L0_attn_a_W_V", "a_wo": "L0_attn_a_W_O",
    "b_wq": "L0_attn_b_W_Q", "b_wk": "L0_attn_b_W_K",
    "b_wv": "L0_attn_b_W_V", "b_wo": "L0_attn_b_W_O",
    "b_bq": "L0_attn_b_b_Q", "b_bk": "L0_attn_b_b_K",
    "b_bv": "L0_attn_b_b_V", "b_bo": "L0_attn_b_b_O",
    "g_w": "L0_ffn_gate_w", "g_s": "L0_ffn_gate_scale",
    "u_w": "L0_ffn_up_w", "u_s": "L0_ffn_up_scale",
    "d_w": "L0_ffn_down_w", "d_s": "L0_ffn_down_scale",
}


def kernel(**inputs):
    full = {k: np.ascontiguousarray(np.asarray(inputs[v])) for k, v in _NMAP.items()}
    x = full.pop("x")
    nc = build_program()
    in_maps = []
    for b in range(8):
        m = dict(full)
        m["x"] = np.ascontiguousarray(x[b])
        in_maps.append(m)
    res = run_bass_kernel_spmd(nc, in_maps, core_ids=list(range(8)))
    return np.stack([r["y"] for r in res.results]).astype(np.float32)
